# version 08
# baseline (speedup 1.0000x reference)
import numpy as np
from contextlib import ExitStack

import concourse.bass as bass
import concourse.mybir as mybir
from concourse.bass_utils import run_bass_kernel_spmd

F32 = mybir.dt.float32
BF16 = mybir.dt.bfloat16
AF = mybir.ActivationFunctionType
ALU = mybir.AluOpType
AX = mybir.AxisListType.X

C = 2048
T = 2048
NH = 16
CT = 16
HALF = 1024
NE = 16
NPAIR = 11
NCHUNK = 2
TCH = 1024
SPAIR = 22


class Sched:
    ENGS = ("tensor", "vector", "scalar", "gpsimd", "sync")

    def __init__(self, nc, ctx):
        self.nc = nc
        self.stream = {e: [] for e in self.ENGS}
        self.sems = {}
        self.cnt = {}
        self.maxwait = {}
        self.ctx = ctx

    def sem(self, name):
        h = self.ctx.enter_context(self.nc.semaphore(name))
        self.sems[name] = h
        self.cnt[name] = 0

    def _wait(self, eng, sem, val):
        if val is None or val <= 0:
            return
        key = (eng, sem)
        if self.maxwait.get(key, -1) >= val:
            return
        self.maxwait[key] = val
        h = self.sems[sem]
        self.stream[eng].append(lambda e, h=h, v=val: e.wait_ge(h, v))

    def op(self, eng, fn, waits=(), inc=None, amt=1):
        if inc is not None and amt == 16:
            waits = list(waits) + [(inc, self.cnt[inc])]
        for s, v in waits:
            self._wait(eng, s, v)
        if inc is None:
            self.stream[eng].append(fn)
            return None
        h = self.sems[inc]
        self.cnt[inc] += amt
        self.stream[eng].append(lambda e, fn=fn, h=h, a=amt: fn(e).then_inc(h, a))
        return self.cnt[inc]

    def emit(self, block):
        for name in self.ENGS:
            ops = list(self.stream[name])

            def make(ops):
                def run(e):
                    for f in ops:
                        f(e)
                return run
            getattr(block, name)(make(ops))


def build_nc(taps=False):
    nc = bass.Bass("TRN2", target_bir_lowering=False, debug=False)

    xT = nc.dram_tensor("xT", [C, T], F32, kind="ExternalInput").ap()
    kvr_w = nc.dram_tensor("kvr_w", [3, NH, 128, C], F32, kind="ExternalInput").ap()
    wo_w = nc.dram_tensor("wo_w", [CT, 128, C], F32, kind="ExternalInput").ap()
    tdec = nc.dram_tensor("tdec", [128, NH], F32, kind="ExternalInput").ap()
    tfirst = nc.dram_tensor("tfirst", [128, NH], F32, kind="ExternalInput").ap()
    gate_w = nc.dram_tensor("gate_w", [128, 256], F32, kind="ExternalInput").ap()
    up_w = nc.dram_tensor("up_w", [NE, NPAIR, 128, 4096], F32, kind="ExternalInput").ap()
    dn_w = nc.dram_tensor("dn_w", [NE, CT, 128, 1408], F32, kind="ExternalInput").ap()
    sup_w = nc.dram_tensor("sup_w", [SPAIR, 128, 4096], F32, kind="ExternalInput").ap()
    sdn_w = nc.dram_tensor("sdn_w", [2, CT, 128, 1408], F32, kind="ExternalInput").ap()
    onehot = nc.dram_tensor("onehot", [16, 2048], F32, kind="ExternalInput").ap()
    ident = nc.dram_tensor("ident", [128, 128], F32, kind="ExternalInput").ap()

    xmid = nc.dram_tensor("xmid", [C, T], F32, kind="Internal").ap()
    outT = nc.dram_tensor("outT", [C, T], F32, kind="ExternalOutput").ap()
    if taps:
        dln1 = nc.dram_tensor("dln1", [128, CT * T], BF16, kind="ExternalOutput").ap()
        drwkv = nc.dram_tensor("drwkv", [128, NH * T], BF16, kind="ExternalOutput").ap()
        dsw = nc.dram_tensor("dsw", [128, HALF], F32, kind="ExternalOutput").ap()
        dwv = nc.dram_tensor("dwv", [128, HALF], F32, kind="ExternalOutput").ap()
        dnum = nc.dram_tensor("dnum", [128, HALF], F32, kind="ExternalOutput").ap()
        dden = nc.dram_tensor("dden", [128, HALF], F32, kind="ExternalOutput").ap()

    xT_t = xT.rearrange("(a p) t -> a p t", p=128)
    xmid_t = xmid.rearrange("(a p) t -> a p t", p=128)
    outT_t = outT.rearrange("(a p) t -> a p t", p=128)

    with ExitStack() as ctx:
        ARENA = 207 * 1024
        arena = nc.alloc_sbuf_tensor("arena", [128, ARENA], mybir.dt.uint8)
        base = nc.lookup_mloc(arena).addr
        off = [0]
        offsets = {}

        def place(name, free_elems, dtype):
            esz = 2 if dtype == BF16 else 4
            offsets[name] = off[0]
            t = nc.alloc_sbuf_tensor_at(name, [128, free_elems], dtype,
                                        offset=base + off[0])
            off[0] += free_elems * esz
            off[0] = (off[0] + 31) & ~31
            return t

        ident_sb = place("ident_sb", 128, F32)
        tdec_sb = place("tdec_sb", NH, F32)
        tfirst_sb = place("tfirst_sb", NH, F32)
        decay_sb = place("decay_sb", NH, F32)
        first_sb = place("first_sb", NH, F32)
        ones_bf = place("ones_bf", 128, BF16)
        gate_bf = place("gate_bf", 256, BF16)
        cst = place("cst", 4, F32)
        onehot_sb = place("onehot_sb", 2048, F32)
        persist_end = off[0]

        off[0] = persist_end
        ln1 = place("ln1", CT * T, BF16)
        rwkv = place("rwkv", NH * T, BF16)
        hsc = place("hsc", 8 * HALF, F32)
        mM = nc.alloc_sbuf_tensor_at("mM", [128, T], F32,
                                     offset=base + offsets["rwkv"] + 2 * T * 2)
        mI = nc.alloc_sbuf_tensor_at("mI", [128, T], F32,
                                     offset=base + offsets["rwkv"] + 2 * T * 2 + T * 4)
        kvrb = place("kvrb", 4 * 2048, BF16)
        wob = place("wob", 2 * 2048, BF16)
        endA = off[0]
        a3x = hsc

        off[0] = persist_end
        ln2 = place("ln2", CT * TCH, BF16)
        oacc = place("oacc", CT * TCH, F32)
        abuf = place("abuf", NPAIR * TCH, BF16)
        upb = place("upb", 2 * 4096, BF16)
        dnb = place("dnb", 2 * 2816, BF16)
        cwe = place("cwe", TCH, F32)
        sgb = place("sgb", TCH, F32)
        ucwb = place("ucwb", TCH, F32)
        m2M = place("m2M", TCH, F32)
        m2I = place("m2I", TCH, F32)
        scs = place("scs", TCH, F32)
        cwtok = place("cwtok", 8 * 16, F32)
        gsc = place("gsc", 512, F32)
        cwT_sb = place("cwT_sb", TCH, F32)
        psg = place("psg", TCH, F32)
        endB = off[0]
        off[0] = max(endA, endB)
        xbf = place("xbf", 2 * TCH, BF16)
        x2bf = place("x2bf", 2 * TCH, BF16)
        endD = off[0]
        assert endD <= ARENA, (endA, endB, endD)

        psum = ctx.enter_context(nc.psum_tensor("ps", [128, 4096], F32))

        S = Sched(nc, ctx)
        for s in ("VC", "SC", "MM",
                  "KV0", "KV1", "WO0", "WO1", "UP0", "UP1", "DN0", "DN1",
                  "XF0", "XF1", "XB0", "XB1", "XB2", "XB3", "SX0", "SX1", "SO0", "SO1",
                  "MISC", "OI0", "OI1", "OI2", "OI3", "PC"):
            S.sem(s)

        V, A_, P, SY, TE = "vector", "scalar", "gpsimd", "sync", "tensor"

        def vop(fn, waits=()):
            w = list(waits) + [("VC", S.cnt["VC"])]
            return S.op(V, fn, waits=w, inc="VC")

        def aop(fn, waits=()):
            w = list(waits) + [("SC", S.cnt["SC"])]
            return S.op(A_, fn, waits=w, inc="SC")

        def pop_(fn, waits=()):
            w = list(waits) + [("PC", S.cnt["PC"])]
            return S.op(P, fn, waits=w, inc="PC")

        pread = {i: ("VC", 0) for i in range(4)}
        pwrit = {i: 0 for i in range(4)}

        def regions(c0, c1):
            return [i for i in range(4) if c0 < (i + 1) * 1024 and c1 > i * 1024]

        def mm(fn, c0, c1, waits=(), end=False):
            w = list(waits) + [pread[r] for r in regions(c0, c1)]
            n = S.op(TE, fn, waits=w, inc="MM" if end else None)
            if end:
                for r in regions(c0, c1):
                    pwrit[r] = n
            return n

        def pconsume(kind, fn, c0, c1, waits=()):
            w = list(waits) + [("MM", max(pwrit[r] for r in regions(c0, c1)))]
            n = vop(fn, w) if kind == "v" else aop(fn, w)
            sem = "VC" if kind == "v" else "SC"
            for r in regions(c0, c1):
                pread[r] = (sem, n)
            return n

        S.op(SY, lambda e: e.dma_start(ident_sb[:, :], ident), inc="MISC", amt=16)
        S.op(SY, lambda e: e.dma_start(tdec_sb[:, :], tdec), inc="MISC", amt=16)
        S.op(SY, lambda e: e.dma_start(tfirst_sb[:, :], tfirst), inc="MISC", amt=16)
        S.op(SY, lambda e: e.dma_start(onehot_sb[0:16, :], onehot), inc="MISC", amt=16)
        misc_w = ("MISC", 64)
        gate_rdy = S.op(P, lambda e: e.dma_start(gate_bf[:, :], gate_w),
                        inc="UP0", amt=16)

        aop(lambda e: e.activation(decay_sb[:, :], tdec_sb[:, :], AF.Exp), waits=[misc_w])
        aop(lambda e: e.activation(decay_sb[:, :], decay_sb[:, :], AF.Exp, scale=-1.0))
        decay_rdy = aop(lambda e: e.activation(first_sb[:, :], tfirst_sb[:, :], AF.Exp))
        ones_rdy = vop(lambda e: e.memset(ones_bf[:, :], 1.0 / 2048.0))
        vop(lambda e: e.memset(cst[:, 0:1], 1e-5))
        vop(lambda e: e.memset(cst[:, 1:2], 1e-6))
        vop(lambda e: e.memset(cst[:, 2:3], 1.0))
        cst_rdy = vop(lambda e: e.memset(cst[:, 3:4], 1e-20))
        cstw = ("VC", cst_rdy)

        xb_rdy = []
        sq_done = [0] * CT
        ct_mm = [0] * CT
        sq_mm = [0] * CT
        for ct in range(CT):
            par = ct % 2
            sem = ("XB0", "XB1", "XB2", "XB3")[ct % 4]
            slot = ln1[:, par * T:(par + 1) * T]
            w = []
            if ct >= 2:
                w = [("MM", ct_mm[ct - 2]), ("SC", sq_done[ct - 2])]
            n = S.op(P, lambda e, d=slot, r=ct: e.dma_start(d, xT_t[r]),
                     waits=w, inc=sem, amt=16)
            xb_rdy.append((sem, n))
            xslot = slot
            x2slot = rwkv[:, par * T:(par + 1) * T]
            sqw = [(sem, n)]
            if ct >= 2:
                sqw.append(("MM", sq_mm[ct - 2]))
            sq_done[ct] = aop(
                lambda e, o=x2slot, i=xslot: e.activation(o, i, AF.Square),
                waits=sqw)
            for tc in range(4):
                last_sum = (tc == 3)
                mm(lambda e, o=psum[:, tc * 512:(tc + 1) * 512],
                   r=xslot[:, tc * 512:(tc + 1) * 512], st=(ct == 0), sp=(ct == CT - 1):
                   e.matmul(o, ones_bf[:, :], r, start=st, stop=sp),
                   tc * 512, (tc + 1) * 512,
                   waits=[(sem, n), ("VC", ones_rdy)], end=last_sum)
                if last_sum:
                    ct_mm[ct] = S.cnt["MM"]
            for tc in range(4):
                mm(lambda e, o=psum[:, 2048 + tc * 512:2048 + (tc + 1) * 512],
                   r=x2slot[:, tc * 512:(tc + 1) * 512], st=(ct == 0), sp=(ct == CT - 1):
                   e.matmul(o, ones_bf[:, :], r, start=st, stop=sp),
                   2048 + tc * 512, 2048 + (tc + 1) * 512,
                   waits=[("SC", sq_done[ct])], end=(tc == 3))
            sq_mm[ct] = S.cnt["MM"]
        stats_mm = S.cnt["MM"]
        pconsume("v", lambda e: e.tensor_copy(mM[:, :], psum[:, 0:2048]), 0, 2048)
        vop(lambda e: e.tensor_mul(mI[:, :], mM[:, :], mM[:, :]))
        pconsume("v", lambda e: e.tensor_sub(mI[:, :], psum[:, 2048:4096], mI[:, :]),
                 2048, 4096)
        aop(lambda e: e.activation(mI[:, :], mI[:, :], AF.Ln, bias=cst[:, 0:1]),
            waits=[("VC", S.cnt["VC"]), cstw])
        inv_rdy = aop(lambda e: e.activation(mI[:, :], mI[:, :], AF.Exp, scale=-0.5))

        kvr_free = [("MM", 0)] * 4
        kvr_pre = {}
        for idx in range(4):
            hh, pj = divmod(idx, 3)
            par = idx % 4
            sem = "KV0" if idx % 2 == 0 else "KV1"
            slot = kvrb[:, par * 2048:(par + 1) * 2048]
            n = S.op(P, lambda e, d=slot, p=pj, hx=hh: e.dma_start(d, kvr_w[p, hx]),
                     waits=[kvr_free[par]], inc=sem, amt=16)
            kvr_pre[(hh, pj)] = (sem, n, slot, par)
        ln1_rdy = [None] * CT
        slot_free = {}
        for ct in range(CT):
            on_pool = ct >= 13
            if on_pool:
                skey = "pool"
                slot = hsc[:, 4096:6144]
                sem = "XF0"
            else:
                skey = ct % 2
                slot = a3x[:, (ct % 2) * 2048:(ct % 2 + 1) * 2048]
                sem = "XF0" if ct % 2 == 0 else "XF1"
            w = [slot_free[skey]] if skey in slot_free else []
            n = S.op(SY, lambda e, d=slot, r=ct: e.dma_start(d, xT_t[r]),
                     waits=w, inc=sem, amt=16)
            eng = pop_ if on_pool else vop
            eng(lambda e, xs=slot: e.tensor_sub(xs, xs, mM[:, :]),
                waits=[(sem, n), ("SC", inv_rdy), ("MM", stats_mm)])
            c = eng(lambda e, o=ln1[:, ct * T:(ct + 1) * T], xs=slot:
                    e.tensor_mul(o, xs, mI[:, :]))
            ln1_rdy[ct] = ("PC" if on_pool else "VC", c)
            slot_free[skey] = ln1_rdy[ct]

        def HS_(i):
            return hsc[:, i * HALF:(i + 1) * HALF]
        s_w, s_wv, s_num, s_den, s_tmp, s_sig, s_dec, _s_sp = [HS_(i) for i in range(8)]

        kvr_info = {}
        rwkv_rdy = [[0, 0] for _ in range(NH)]
        seq = 0
        for h in range(NH):
            for pj in range(3):
                if (h, pj) in kvr_pre:
                    kvr_info[pj] = kvr_pre[(h, pj)]
                    continue
                idx = h * 3 + pj
                par = idx % 4
                sem = "KV0" if idx % 2 == 0 else "KV1"
                slot = kvrb[:, par * 2048:(par + 1) * 2048]
                n = S.op(P, lambda e, d=slot, p=pj, hh=h: e.dma_start(d, kvr_w[p, hh]),
                         waits=[kvr_free[par]], inc=sem, amt=16)
                kvr_info[pj] = (sem, n, slot, par)
            dcol = decay_sb[:, h:h + 1]
            dbc = bass.AP(dcol.tensor, dcol.offset, [list(dcol.ap[0]), [0, HALF]])
            grp = {}
            for half in range(2):
                for pj in range(3):
                    sem, n, slot, par = kvr_info[pj]
                    sl = seq % 4
                    seq += 1
                    c0 = sl * 1024
                    for dt in range(16):
                        for tc in range(2):
                            t0 = half * 1024 + tc * 512
                            mm(lambda e, o=psum[:, c0 + tc * 512:c0 + (tc + 1) * 512],
                               l=slot[:, dt * 128:(dt + 1) * 128],
                               r=ln1[:, dt * T + t0:dt * T + t0 + 512],
                               st=(dt == 0), sp=(dt == 15):
                               e.matmul(o, l, r, start=st, stop=sp),
                               c0 + tc * 512, c0 + (tc + 1) * 512,
                               waits=[(sem, n), ln1_rdy[dt]],
                               end=(dt == 15 and tc == 1))
                    grp[(pj, half)] = (S.cnt["MM"], c0)
                    if half == 1:
                        kvr_free[par] = ("MM", grp[(pj, 1)][0])
                kmm, kc0 = grp[(0, half)]
                vmm, vc0 = grp[(1, half)]
                rmm, rc0 = grp[(2, half)]
                pconsume("v", lambda e, c=kc0: e.tensor_scalar(
                    s_w, psum[:, c:c + 1024], 10.0, -10.0, ALU.min, ALU.max),
                    kc0, kc0 + 1024, waits=[("SC", S.cnt["SC"])])
                we = aop(lambda e: e.activation(s_w, s_w, AF.Exp),
                         waits=[("VC", S.cnt["VC"])])
                if half == 0:
                    we = aop(lambda e, hh=h: e.mul(s_w[:, 0:1], s_w[:, 0:1],
                                                   first_sb[:, hh:hh + 1]))
                pconsume("v", lambda e, c=vc0: e.tensor_mul(s_wv, s_w, psum[:, c:c + 1024]),
                         vc0, vc0 + 1024, waits=[("SC", we)])
                init_n = 0.0 if half == 0 else s_num[:, HALF - 1:HALF]
                init_d = 0.0 if half == 0 else s_den[:, HALF - 1:HALF]
                vop(lambda e, ini=init_n, d=dbc: e.tensor_tensor_scan(
                    s_num, d, s_wv, ini, ALU.mult, ALU.add),
                    waits=[("SC", decay_rdy)])
                dsc = vop(lambda e, ini=init_d, d=dbc: e.tensor_tensor_scan(
                    s_den, d, s_w, ini, ALU.mult, ALU.add))
                aop(lambda e: e.activation(s_tmp, s_den, AF.Ln, bias=cst[:, 1:2]),
                    waits=[("VC", dsc), cstw])
                rec = aop(lambda e: e.activation(s_tmp, s_tmp, AF.Exp, scale=-1.0))
                yv = vop(lambda e: e.tensor_mul(s_wv, s_num, s_tmp), waits=[("SC", rec)])
                prev_rw = rwkv_rdy[h][0] if half == 1 else (
                    rwkv_rdy[h - 1][1] if h > 0 else 0)
                pconsume("a", lambda e, c=rc0: e.activation(
                    s_sig, psum[:, c:c + 1024], AF.Exp, scale=-1.0),
                    rc0, rc0 + 1024, waits=[("VC", prev_rw)])
                aop(lambda e: e.activation(s_sig, s_sig, AF.Ln, bias=cst[:, 2:3]),
                    waits=[cstw])
                sgm = aop(lambda e: e.activation(s_sig, s_sig, AF.Exp, scale=-1.0))
                rwkv_rdy[h][half] = vop(
                    lambda e, o=rwkv[:, h * T + half * 1024:h * T + half * 1024 + 1024]:
                    e.tensor_mul(o, s_sig, s_wv), waits=[("SC", sgm)])
                if taps and h == 0 and half == 0:
                    S.op(SY, lambda e: e.dma_start(dsw, s_w), waits=[("SC", we)],
                         inc="MISC", amt=16)
                    S.op(SY, lambda e: e.dma_start(dnum, s_num),
                         waits=[("VC", S.cnt["VC"])], inc="MISC", amt=16)
                    S.op(SY, lambda e: e.dma_start(dden, s_den),
                         waits=[("VC", dsc)], inc="MISC", amt=16)
                    nmisc = S.cnt["MISC"]
                    S._wait("vector", "MISC", nmisc)
                    S._wait("scalar", "MISC", nmisc)

        wo_free = [("MM", 0), ("MM", 0)]
        last_head_vc = rwkv_rdy[NH - 1][1]
        sc_a3 = S.cnt["SC"]
        pc_a3 = S.cnt["PC"]
        x3_free = [last_head_vc, last_head_vc]
        o3_free = [None, None]
        for j in range(CT):
            par = j % 2
            wsem = "WO0" if par == 0 else "WO1"
            wslot = wob[:, par * 2048:(par + 1) * 2048]
            wn = S.op(P, lambda e, d=wslot, r=j: e.dma_start(d, wo_w[r]),
                      waits=[wo_free[par]], inc=wsem, amt=16)
            xsem = "XF0" if par == 0 else "XF1"
            xslot = a3x[:, par * 2048:(par + 1) * 2048]
            xdw = [("VC", x3_free[par])]
            if j < 2:
                xdw.append(("SC", sc_a3))
                xdw.append(("PC", pc_a3))
            xn = S.op(SY, lambda e, d=xslot, r=j: e.dma_start(d, xT_t[r]),
                      waits=xdw, inc=xsem, amt=16)
            c0 = par * 2048
            for hh in range(NH):
                for tc in range(4):
                    half = tc // 2
                    mm(lambda e, o=psum[:, c0 + tc * 512:c0 + (tc + 1) * 512],
                       l=wslot[:, hh * 128:(hh + 1) * 128],
                       r=rwkv[:, hh * T + tc * 512:hh * T + (tc + 1) * 512],
                       st=(hh == 0), sp=(hh == NH - 1):
                       e.matmul(o, l, r, start=st, stop=sp),
                       c0 + tc * 512, c0 + (tc + 1) * 512,
                       waits=[(wsem, wn), ("VC", rwkv_rdy[hh][half])],
                       end=(hh == NH - 1 and tc == 3))
            gmm = S.cnt["MM"]
            wo_free[par] = ("MM", gmm)
            oslot = hsc[:, 4096 + par * 2048:4096 + (par + 1) * 2048]
            ow = [(xsem, xn), ("SC", sc_a3)]
            if o3_free[par] is not None:
                ow.append(o3_free[par])
            vn = pconsume("v", lambda e, o=oslot, x=xslot, c=c0:
                          e.tensor_add(o, psum[:, c:c + 2048], x), c0, c0 + 2048,
                          waits=ow)
            x3_free[par] = vn
            sxs = "SX0" if par == 0 else "SX1"
            S.op(SY, lambda e, s=oslot, r=j: e.dma_start(xmid_t[r], s),
                 waits=[("VC", vn)], inc=sxs, amt=16)
            o3_free[par] = (sxs, 16 * (j // 2 + 1))

        if taps:
            S.op(SY, lambda e: e.dma_start(dln1, ln1[:, :]),
                 waits=[ln1_rdy[CT - 1]], inc="MISC", amt=16)
            S.op(SY, lambda e: e.dma_start(drwkv, rwkv[:, :]),
                 waits=[("VC", rwkv_rdy[NH - 1][1])], inc="MISC", amt=16)
        joinA = [("VC", S.cnt["VC"]), ("SC", S.cnt["SC"]), ("MM", S.cnt["MM"]),
                 ("PC", S.cnt["PC"]),
                 ("SX0", 16 * 8), ("SX1", 16 * 8), ("MISC", S.cnt["MISC"])]

        up_dma_i = [0]
        dn_dma_i = [0]
        cw_mm_last = [0]
        up_free = [0, 0]
        dn_free = [0, 0]
        dn_rdy = [None] * CT

        for ch in range(NCHUNK):
            t0 = ch * TCH
            chw = list(joinA) if ch == 0 else []
            xb_rdy = []
            sqd = [0] * CT
            ctm = [0] * CT
            sqm = [0] * CT
            for ct in range(CT):
                par = ct % 2
                sem = ("XB0", "XB1", "XB2", "XB3")[ct % 4]
                slot = xbf[:, par * TCH:(par + 1) * TCH]
                if ch == 0:
                    w = [("SX0" if ct % 2 == 0 else "SX1", 16 * (ct // 2 + 1))]
                else:
                    w = []
                if ct >= 2:
                    w += [("MM", ctm[ct - 2]), ("SC", sqd[ct - 2])]
                n = S.op(P, lambda e, d=slot, r=ct, tt=t0: e.dma_start(
                    d, xmid_t[r][:, tt:tt + TCH]), waits=w, inc=sem, amt=16)
                xb_rdy.append((sem, n))
                x2slot = x2bf[:, par * TCH:(par + 1) * TCH]
                sqw = [(sem, n)]
                if ct >= 2:
                    sqw.append(("MM", sqm[ct - 2]))
                sqd[ct] = aop(lambda e, o=x2slot, i=slot: e.activation(o, i, AF.Square),
                              waits=sqw)
                for tc in range(2):
                    mm(lambda e, o=psum[:, tc * 512:(tc + 1) * 512],
                       r=slot[:, tc * 512:(tc + 1) * 512], st=(ct == 0), sp=(ct == CT - 1):
                       e.matmul(o, ones_bf[:, :], r, start=st, stop=sp),
                       tc * 512, (tc + 1) * 512,
                       waits=[(sem, n), ("VC", ones_rdy)] + list(chw), end=(tc == 1))
                    if tc == 1:
                        ctm[ct] = S.cnt["MM"]
                for tc in range(2):
                    mm(lambda e, o=psum[:, 1024 + tc * 512:1024 + (tc + 1) * 512],
                       r=x2slot[:, tc * 512:(tc + 1) * 512], st=(ct == 0), sp=(ct == CT - 1):
                       e.matmul(o, ones_bf[:, :], r, start=st, stop=sp),
                       1024 + tc * 512, 1024 + (tc + 1) * 512,
                       waits=[("SC", sqd[ct])], end=(tc == 1))
                sqm[ct] = S.cnt["MM"]
            pconsume("v", lambda e: e.tensor_copy(m2M[:, :], psum[:, 0:1024]), 0, 1024,
                     waits=chw)
            vop(lambda e: e.tensor_mul(m2I[:, :], m2M[:, :], m2M[:, :]))
            pconsume("v", lambda e: e.tensor_sub(m2I[:, :], psum[:, 1024:2048], m2I[:, :]),
                     1024, 2048)
            aop(lambda e: e.activation(m2I[:, :], m2I[:, :], AF.Ln, bias=cst[:, 0:1]),
                waits=[("VC", S.cnt["VC"]), cstw] + list(chw))
            inv2 = aop(lambda e: e.activation(m2I[:, :], m2I[:, :], AF.Exp, scale=-0.5))

            ln2_rdy = [None] * CT
            for ct in range(CT):
                sem = ("OI0", "OI1", "OI2", "OI3")[ct % 4]
                dst = oacc[:, ct * TCH:(ct + 1) * TCH]
                w = list(chw)
                if ch > 0:
                    w.append(("SO0" if ct % 2 == 0 else "SO1",
                              16 * ((ch - 1) * (CT // 2) + ct // 2 + 1)))
                n = S.op(SY, lambda e, d=dst, r=ct, tt=t0: e.dma_start(
                    d, xmid_t[r][:, tt:tt + TCH]), waits=w, inc=sem, amt=16)
                vop(lambda e, xs=dst: e.tensor_sub(sgb[:, :], xs, m2M[:, :]),
                    waits=[(sem, n), ("SC", inv2)] + list(chw))
                c = vop(lambda e, o=ln2[:, ct * TCH:(ct + 1) * TCH]:
                        e.tensor_mul(o, sgb[:, :], m2I[:, :]))
                ln2_rdy[ct] = ("VC", c)

            for dt in range(16):
                for tc in range(2):
                    mm(lambda e, o=psum[0:16, tc * 512:(tc + 1) * 512],
                       l=gate_bf[:, dt * 16:(dt + 1) * 16],
                       r=ln2[:, dt * TCH + tc * 512:dt * TCH + (tc + 1) * 512],
                       st=(dt == 0), sp=(dt == 15):
                       e.matmul(o, l, r, start=st, stop=sp),
                       tc * 512, (tc + 1) * 512,
                       waits=[("UP0", gate_rdy), ln2_rdy[dt]],
                       end=(dt == 15 and tc == 1))
            sc_ev = pconsume("a", lambda e: e.copy(scs[0:16, :], psum[0:16, 0:1024]),
                             0, 1024, waits=[misc_w])
            for tt in range(8):
                mm(lambda e, o=psum[:, 2048 + tt * 16:2048 + (tt + 1) * 16],
                   i=scs[0:16, tt * 128:(tt + 1) * 128]:
                   e.transpose(o, i, ident_sb[0:16, 0:16]),
                   2048 + tt * 16, 2048 + (tt + 1) * 16,
                   waits=[("SC", sc_ev), misc_w], end=(tt == 7))
            cwt_ev = pconsume("v", lambda e: e.tensor_copy(cwtok[:, :], psum[:, 2048:2176]),
                              2048, 2176)
            def B8(t, inner):
                ap = t
                return bass.AP(ap.tensor, ap.offset,
                               list(ap.ap) + [[0, inner]])
            rm8 = gsc[:, 0:8]
            p8 = gsc[:, 16:144]
            gs8 = gsc[:, 144:176]
            m8 = gsc[:, 176:184]
            gm8 = gsc[:, 184:216]
            cur8 = gsc[:, 216:344]
            sel8 = gsc[:, 344:472]
            tmp4 = gsc[:, 472:504]
            ss8 = gsc[:, 8:16]
            rs8 = gsc[:, 504:512]
            scv = cwtok[:, :].rearrange("p (a b) -> p a b", a=8)
            p8v = p8.rearrange("p (a b) -> p a b", a=8)
            p8q = p8.rearrange("p (a g j) -> p a g j", a=8, g=4)
            gs8v = gs8.rearrange("p (a g) -> p a g", a=8)
            gm8v = gm8.rearrange("p (a g) -> p a g", a=8)
            cur8v = cur8.rearrange("p (a b) -> p a b", a=8)
            sel8v = sel8.rearrange("p (a b) -> p a b", a=8)
            tmp4v = tmp4.rearrange("p (a g) -> p a g", a=8)
            rm8v = rm8.rearrange("p a -> p a")
            vop(lambda e: e.tensor_reduce(rm8[:, :], scv, axis=AX, op=ALU.max),
                waits=[("VC", cwt_ev)])
            v2 = vop(lambda e: e.tensor_tensor(p8v, scv, B8(rm8v, 16), ALU.subtract))
            a1 = aop(lambda e: e.activation(p8, p8, AF.Exp), waits=[("VC", v2)])
            vop(lambda e: e.tensor_reduce(gs8v, p8q, axis=AX, op=ALU.max),
                waits=[("SC", a1)])
            vop(lambda e: e.tensor_reduce(m8[:, :], gs8v, axis=AX, op=ALU.max))
            vop(lambda e: e.tensor_tensor(gm8v, gs8v, B8(m8, 4), ALU.is_ge))
            vop(lambda e: e.scalar_tensor_tensor(tmp4v, gm8v, -1e30, gs8v,
                                                 ALU.mult, ALU.add))
            vop(lambda e: e.tensor_reduce(m8[:, :], tmp4v, axis=AX, op=ALU.max))
            vop(lambda e: e.tensor_tensor(gm8v, gs8v, B8(m8, 4), ALU.is_ge))
            vop(lambda e: e.tensor_tensor(p8q, p8q, B8(gm8v, 4), ALU.mult))
            vop(lambda e: e.tensor_copy(cur8[:, :], p8))
            for _ in range(3):
                vop(lambda e: e.tensor_reduce(m8[:, :], cur8v, axis=AX, op=ALU.max))
                vop(lambda e: e.tensor_tensor(sel8v, cur8v, B8(m8, 16), ALU.is_ge))
                vop(lambda e: e.scalar_tensor_tensor(cur8v, sel8v, -1e30, cur8v,
                                                     ALU.mult, ALU.add))
            vop(lambda e: e.tensor_reduce(m8[:, :], cur8v, axis=AX, op=ALU.max))
            vop(lambda e: e.tensor_tensor(sel8v, p8v, B8(m8, 16), ALU.is_ge))
            vop(lambda e: e.tensor_tensor(p8v, p8v, sel8v, ALU.mult))
            v23 = vop(lambda e: e.tensor_reduce(ss8[:, :], p8v, axis=AX, op=ALU.add))
            aop(lambda e: e.activation(rs8, ss8, AF.Ln, bias=cst[:, 3:4]),
                waits=[("VC", v23), cstw])
            a3 = aop(lambda e: e.activation(rs8, rs8, AF.Exp, scale=-1.0))
            cw_fin = vop(lambda e: e.tensor_tensor(scv, p8v, B8(rs8, 16), ALU.mult),
                         waits=[("SC", a3)])

            def emit_cwback():
                for tt in range(8):
                    mm(lambda e, o=psum[0:16, 2048 + tt * 128:2048 + (tt + 1) * 128],
                       i=cwtok[:, tt * 16:(tt + 1) * 16]:
                       e.transpose(o, i, ident_sb[:, :]),
                       2048 + tt * 128, 2048 + (tt + 1) * 128,
                       waits=[("VC", cw_fin), ("MM", cw_mm_last[0])], end=(tt == 7))
                return pconsume("a", lambda e: e.copy(cwT_sb[0:16, :],
                                                      psum[0:16, 2048:3072]),
                                2048, 3072)
            cwT_ev_ref = [None]

            a_rdy = [0] * NPAIR
            silu_last = [0]
            amul_last = [0]
            oacc_last = [ln2_rdy[ct] for ct in range(CT)]

            def do_expert(kind, e_idx, hf=0):
                real = kind == "e"
                cw_rdy = [0]
                pair_mm = [0] * NPAIR
                for p in range(NPAIR):
                    par = up_dma_i[0] % 2
                    sem = "UP0" if par == 0 else "UP1"
                    up_dma_i[0] += 1
                    slot = upb[:, par * 4096:(par + 1) * 4096]
                    src = up_w[e_idx, p] if real else sup_w[hf * NPAIR + p]
                    n = S.op(P, lambda e, d=slot, s=src: e.dma_start(d, s),
                             waits=[("MM", up_free[par])], inc=sem, amt=16)
                    gc0 = 0 if p % 2 == 0 else 2048
                    uc0 = 1024 if p % 2 == 0 else 3072
                    for dt in range(16):
                        for tc in range(2):
                            mm(lambda e, o=psum[:, gc0 + tc * 512:gc0 + (tc + 1) * 512],
                               l=slot[:, dt * 256:dt * 256 + 128],
                               r=ln2[:, dt * TCH + tc * 512:dt * TCH + (tc + 1) * 512],
                               st=(dt == 0), sp=(dt == 15):
                               e.matmul(o, l, r, start=st, stop=sp),
                               gc0 + tc * 512, gc0 + (tc + 1) * 512,
                               waits=[(sem, n), ln2_rdy[dt]],
                               end=(dt == 15 and tc == 1))
                        for tc in range(2):
                            mm(lambda e, o=psum[:, uc0 + tc * 512:uc0 + (tc + 1) * 512],
                               l=slot[:, dt * 256 + 128:dt * 256 + 256],
                               r=ln2[:, dt * TCH + tc * 512:dt * TCH + (tc + 1) * 512],
                               st=(dt == 0), sp=(dt == 15):
                               e.matmul(o, l, r, start=st, stop=sp),
                               uc0 + tc * 512, uc0 + (tc + 1) * 512,
                               waits=[(sem, n)],
                               end=(dt == 15 and tc == 1))
                    pair_mm[p] = S.cnt["MM"]
                    up_free[par] = pair_mm[p]
                    if real and p == 0:
                        if cwT_ev_ref[0] is None or True:
                            pass
                        if e_idx == 0:
                            cwT_ev_ref[0] = emit_cwback()
                        for tc in range(2):
                            n2 = mm(lambda e, o=psum[:, 3072 + tc * 512:3072 + (tc + 1) * 512],
                                    l=onehot_sb[0:16, e_idx * 128:(e_idx + 1) * 128],
                                    r=cwT_sb[0:16, tc * 512:(tc + 1) * 512]:
                                    e.matmul(o, l, r, start=True, stop=True),
                                    3072 + tc * 512, 3072 + (tc + 1) * 512,
                                    waits=[("SC", cwT_ev_ref[0]), misc_w], end=(tc == 1))
                        cw_mm_last[0] = n2
                        cw_rdy[0] = pconsume(
                            "a", lambda e: e.copy(cwe[:, :], psum[:, 3072:4096]),
                            3072, 4096, waits=[("VC", amul_last[0])])
                    silu_last[0] = pconsume(
                        "a", lambda e, c=gc0: e.activation(sgb[:, :], psum[:, c:c + 1024],
                                                           AF.Silu),
                        gc0, gc0 + 1024, waits=[("VC", amul_last[0])])
                    if real:
                        pconsume("v", lambda e, c=uc0: e.tensor_mul(
                            ucwb[:, :], psum[:, c:c + 1024], cwe[:, :]),
                            uc0, uc0 + 1024, waits=[("SC", cw_rdy[0])])
                        amul_last[0] = vop(
                            lambda e, o=abuf[:, p * TCH:(p + 1) * TCH]:
                            e.tensor_mul(o, sgb[:, :], ucwb[:, :]),
                            waits=[("SC", silu_last[0])])
                    else:
                        amul_last[0] = pconsume(
                            "v", lambda e, c=uc0, o=abuf[:, p * TCH:(p + 1) * TCH]:
                            e.tensor_mul(o, sgb[:, :], psum[:, c:c + 1024]),
                            uc0, uc0 + 1024, waits=[("SC", silu_last[0])])
                    a_rdy[p] = amul_last[0]
                def issue_dn(dmac, w):
                    par = dn_dma_i[0] % 2
                    sem = "DN0" if par == 0 else "DN1"
                    dn_dma_i[0] += 1
                    slot = dnb[:, par * 2816:(par + 1) * 2816]
                    src = (dn_w[e_idx, 2 * dmac:2 * dmac + 2] if real
                           else sdn_w[hf, 2 * dmac:2 * dmac + 2])
                    S.op(P, lambda e, d=slot, s=src: e.dma_start(
                        d.rearrange("p (a f) -> p a f", a=2),
                        s.rearrange("a p f -> p a f")),
                        waits=w, inc=sem, amt=16)
                    dn_rdy[2 * dmac] = (sem, S.cnt[sem])
                    dn_rdy[2 * dmac + 1] = (sem, S.cnt[sem])

                issue_dn(0, [("MM", dn_free[0])])
                issue_dn(1, [("MM", dn_free[1])])
                for ct in range(CT):
                    par2 = (ct // 2) % 2
                    woff = par2 * 2816 + (ct % 2) * 1408
                    c0 = (ct % 2) * 1024
                    for it in range(NPAIR):
                        for tc in range(2):
                            mm(lambda e, o=psum[:, c0 + tc * 512:c0 + (tc + 1) * 512],
                               l=dnb[:, woff + it * 128:woff + (it + 1) * 128],
                               r=abuf[:, it * TCH + tc * 512:it * TCH + (tc + 1) * 512],
                               st=(it == 0), sp=(it == NPAIR - 1):
                               e.matmul(o, l, r, start=st, stop=sp),
                               c0 + tc * 512, c0 + (tc + 1) * 512,
                               waits=[dn_rdy[ct], ("VC", a_rdy[it])],
                               end=(it == NPAIR - 1 and tc == 1))
                    ctmm = S.cnt["MM"]
                    if ct % 2 == 1:
                        blob = ct // 2
                        dn_free[blob % 2] = ctmm
                        if blob + 2 < 8:
                            issue_dn(blob + 2, [("MM", ctmm)])
                    n_ev = pconsume(
                        "v", lambda e, c=c0, o=oacc[:, ct * TCH:(ct + 1) * TCH]:
                        e.tensor_add(o, psum[:, c:c + 1024], o),
                        c0, c0 + 1024, waits=[oacc_last[ct]])
                    oacc_last[ct] = ("VC", n_ev)

            for e_idx in range(NE):
                do_expert("e", e_idx)
            do_expert("s", 0, hf=0)
            do_expert("s", 0, hf=1)

            for ct in range(CT):
                S.op(SY, lambda e, r=ct, tt=t0, s=oacc[:, ct * TCH:(ct + 1) * TCH]:
                     e.dma_start(outT_t[r][:, tt:tt + TCH], s),
                     waits=[oacc_last[ct]],
                     inc="SO0" if ct % 2 == 0 else "SO1", amt=16)

        S._wait(SY, "SO0", 16 * (CT // 2) * NCHUNK)
        S._wait(SY, "SO1", 16 * (CT // 2) * NCHUNK)

        block = ctx.enter_context(nc.Block())
        S.emit(block)

    return nc


def host_layout(L0_attn_x, L0_attn_Wk, L0_attn_Wv, L0_attn_Wr, L0_attn_Wo,
                L0_attn_time_decay, L0_attn_time_first, L0_ffn_gate_w,
                L0_ffn_expert_up_w, L0_ffn_expert_down_w, L0_ffn_shared_up_w,
                L0_ffn_shared_down_w):
    f = np.ascontiguousarray

    def kvr_tile(W):
        V = np.asarray(W, np.float32).reshape(16, 128, 16, 128)
        return V.transpose(0, 3, 2, 1).reshape(16, 128, 2048)

    kvr = np.stack([kvr_tile(L0_attn_Wk), kvr_tile(L0_attn_Wv), kvr_tile(L0_attn_Wr)])
    wo = kvr_tile(L0_attn_Wo)

    up = np.asarray(L0_ffn_expert_up_w, np.float32)
    V = up.reshape(NE, 2, NPAIR, 128, 16, 128)
    upt = V.transpose(0, 2, 5, 4, 1, 3).reshape(NE, NPAIR, 128, 4096)

    dn = np.asarray(L0_ffn_expert_down_w, np.float32)
    V = dn.reshape(NE, 16, 128, NPAIR, 128)
    dnt = V.transpose(0, 1, 4, 3, 2).reshape(NE, 16, 128, 1408)

    sup = np.asarray(L0_ffn_shared_up_w, np.float32)
    V = sup.reshape(2, SPAIR, 128, 16, 128)
    supt = V.transpose(1, 4, 3, 0, 2).reshape(SPAIR, 128, 4096)

    sdn = np.asarray(L0_ffn_shared_down_w, np.float32)
    V = sdn.reshape(16, 128, 2, NPAIR, 128)
    sdnt = V.transpose(2, 0, 4, 3, 1).reshape(2, 16, 128, 1408)

    gw = np.asarray(L0_ffn_gate_w, np.float32)
    gatet = gw.reshape(16, 16, 128).transpose(2, 1, 0).reshape(128, 256)

    oh = np.zeros((16, 16, 128), np.float32)
    for e in range(NE):
        oh[e, e, :] = 1.0
    oh = oh.reshape(16, 2048)

    shared = {
        "kvr_w": f(kvr), "wo_w": f(wo),
        "tdec": f(np.asarray(L0_attn_time_decay, np.float32).T),
        "tfirst": f(np.asarray(L0_attn_time_first, np.float32).T),
        "gate_w": f(gatet), "up_w": f(upt), "dn_w": f(dnt),
        "sup_w": f(supt), "sdn_w": f(sdnt), "onehot": oh,
        "ident": np.eye(128, dtype=np.float32),
    }
    x = np.asarray(L0_attn_x, np.float32)
    in_maps = []
    for b in range(8):
        m = dict(shared)
        m["xT"] = f(x[b].T)
        in_maps.append(m)
    return in_maps


def kernel(**inputs):
    in_maps = host_layout(**inputs)
    nc = build_nc()
    res = run_bass_kernel_spmd(nc, in_maps, core_ids=list(range(8)))
    out = np.stack([np.asarray(res.results[b]["outT"]).T for b in range(8)])
    return np.ascontiguousarray(out)
